# version 03
# baseline (speedup 1.0000x reference)
import contextlib
import os
import numpy as np
import concourse.bass as bass
import concourse.mybir as mybir
from concourse.bass_utils import run_bass_kernel_spmd
from concourse.ap import AP

F32 = mybir.dt.float32
BF16 = mybir.dt.bfloat16
I32 = mybir.dt.int32
AF = mybir.ActivationFunctionType
ALU = mybir.AluOpType
AXX = mybir.AxisListType.X

B, S, HID = 4, 2048, 2048
NH, DK, DV = 16, 128, 128
HPC = 8
CHN = HPC * DK
CSZ = 16
TILE = 256
NTILE = S // TILE
CPT = TILE // CSZ
TOKH = 1024
EPS = 1e-5
BLKE = HPC * 128 * TOKH


def _host_consts():
    import ml_dtypes
    bf = ml_dtypes.bfloat16
    tris = np.zeros((128, 128), np.float32)
    trii = np.zeros((128, 128), np.float32)
    masksw = np.zeros((128, 128), np.float32)
    maskiw = np.zeros((128, 128), np.float32)
    diagones = np.zeros((128, 128), np.float32)
    for h in range(8):
        for tau in range(16):
            for t in range(16):
                r, c = h * 16 + tau, h * 16 + t
                if tau < t:
                    tris[r, c] = 1.0
                    masksw[r, c] = 1.0
                    maskiw[r, c] = 1.0
                if tau <= t:
                    trii[r, c] = 1.0
                if tau == t:
                    maskiw[r, c] = 1.0
                    diagones[r, c] = 1.0
    hmask2 = np.zeros((128, 1024), np.float32)
    for h in range(8):
        hmask2[h * 16:(h + 1) * 16, h * 128:(h + 1) * 128] = 1.0
    return {
        "identb": np.eye(128, dtype=np.float32).astype(bf),
        "identf": np.eye(128, dtype=np.float32),
        "trisb": tris.astype(bf), "triib": trii.astype(bf),
        "masksw": masksw.astype(bf), "maskiw": maskiw.astype(bf),
        "diagones": diagones.astype(bf), "hmask2": hmask2.astype(bf),
        "onesb": np.ones((128, 128), np.float32).astype(bf),
        "ones1f": np.ones((128, 1), np.float32),
        "onesrowf": np.ones((1, 128), np.float32),
        "onesrowb": np.ones((1, 128), np.float32).astype(bf),
        "epscol": np.full((128, 1), 1e-5, np.float32),
    }


def _in_maps(inputs):
    g = lambda k: np.asarray(inputs[k])
    x = g("L0_attn_x").astype(np.float32)
    Wq, Wk, Wv, Wa, Wg, Wo = (g(f"L0_attn_W{c}") for c in "qkvago")
    Wb = g("L0_attn_Wb"); ba = g("L0_attn_ba"); bb = g("L0_attn_bb")
    cw = {c: g(f"L0_attn_c{c}w")[:, 0, :] for c in "qkv"}
    cb = {c: g(f"L0_attn_c{c}b") for c in "qkv"}
    lnw, lnb = g("L0_attn_ln_w"), g("L0_attn_ln_b")
    upw, dnw = g("L0_ffn_up_weight"), g("L0_ffn_down_weight")
    ups, dns, osc = g("L0_ffn_up_scale"), g("L0_ffn_down_scale"), g("L0_ffn_output_scale")
    consts = _host_consts()
    upwT = np.ascontiguousarray(upw.T)
    dnwT = np.ascontiguousarray(dnw.T)
    upsT = np.ascontiguousarray(ups.reshape(64, 128).T)
    maps = []
    for core in range(8):
        b, half = core // 2, core % 2
        cs = slice(half * CHN, (half + 1) * CHN)
        hs = slice(half * HPC, (half + 1) * HPC)
        ts = slice(half * TOKH, (half + 1) * TOKH)
        m = {
            "xT": np.ascontiguousarray(x[b].T),
            "xres": np.ascontiguousarray(x[b, ts]),
            "wq": np.ascontiguousarray(Wq[:, cs]),
            "wk": np.ascontiguousarray(Wk[:, cs]),
            "wv": np.ascontiguousarray(Wv[:, cs]),
            "wa": np.ascontiguousarray(Wa[:, cs]),
            "wg": np.ascontiguousarray(Wg[:, cs]),
            "wb": np.ascontiguousarray(Wb[:, hs]),
            "bacols": np.ascontiguousarray(ba[cs].reshape(8, 128).T),
            "bbcol": np.ascontiguousarray(bb[hs].reshape(8, 1)),
            "cwcols": np.ascontiguousarray(
                np.stack([cw[c][cs].reshape(8, 128, 4) for c in "qkv"])
                .transpose(2, 0, 1, 3).reshape(128, 96)),
            "cbcols": np.ascontiguousarray(
                np.stack([cb[c][cs].reshape(8, 128) for c in "qkv"])
                .transpose(2, 0, 1).reshape(128, 24)),
            "lnwrow": np.ascontiguousarray(lnw.reshape(1, 128)),
            "lnbrow": np.ascontiguousarray(lnb.reshape(1, 128)),
            "wo": Wo,
            "upwT": upwT, "dnwT": dnwT, "upsT": upsT,
            "dnsrow": np.ascontiguousarray(dns.reshape(1, 2048)),
            "osccol": np.ascontiguousarray(osc[np.zeros(128, np.int64)].reshape(128, 1)),
            "exoff": np.array([[half * BLKE, (1 - half) * BLKE]], np.int32),
        }
        m.update(consts)
        maps.append(m)
    return maps


def build_nc():
    PHASE = int(os.environ.get("KPHASE", "9"))
    nc = bass.Bass("TRN2", target_bir_lowering=False,
                   detect_race_conditions=(os.environ.get("KRACE", "0") == "1"))

    d_in = {}
    for name, shape, dt in [
        ("xT", [HID, S], F32), ("xres", [TOKH, HID], F32),
        ("wq", [HID, CHN], F32), ("wk", [HID, CHN], F32), ("wv", [HID, CHN], F32),
        ("wa", [HID, CHN], F32), ("wg", [HID, CHN], F32), ("wb", [HID, 8], F32),
        ("bacols", [128, 8], F32), ("bbcol", [8, 1], F32),
        ("cwcols", [128, 96], F32), ("cbcols", [128, 24], F32),
        ("lnwrow", [1, 128], F32), ("lnbrow", [1, 128], F32),
        ("wo", [HID, HID], F32),
        ("upwT", [HID, 8192], F32), ("dnwT", [8192, HID], F32),
        ("upsT", [128, 64], F32), ("dnsrow", [1, HID], F32), ("osccol", [128, 1], F32),
        ("exoff", [1, 2], I32),
        ("identb", [128, 128], BF16), ("identf", [128, 128], F32),
        ("trisb", [128, 128], BF16), ("triib", [128, 128], BF16),
        ("masksw", [128, 128], BF16), ("maskiw", [128, 128], BF16),
        ("diagones", [128, 128], BF16), ("hmask2", [128, 1024], BF16),
        ("onesb", [128, 128], BF16), ("ones1f", [128, 1], F32),
        ("onesrowf", [1, 128], F32), ("onesrowb", [1, 128], BF16),
        ("epscol", [128, 1], F32),
    ]:
        d_in[name] = nc.dram_tensor(name, shape, dt, kind="ExternalInput").ap()

    out = nc.dram_tensor("out", [TOKH, HID], F32, kind="ExternalOutput").ap()

    d_qs = nc.dram_tensor("qs", [CHN, S], BF16).ap()
    d_ks = nc.dram_tensor("ks", [CHN, S], BF16).ap()
    d_vs = nc.dram_tensor("vs", [CHN, S], BF16).ap()
    d_zas = nc.dram_tensor("zas", [CHN, S], F32).ap()
    d_zgs = nc.dram_tensor("zgs", [CHN, S], BF16).ap()
    d_ogl = nc.dram_tensor("ogl", [2, HPC, 128, TOKH], BF16).ap()
    d_agin = nc.dram_tensor("agin", [HPC, 128, TOKH], BF16).ap()
    d_agout = nc.dram_tensor("agout", [2, HPC, 128, TOKH], BF16).ap()
    d_ogf = nc.dram_tensor("ogf", [NH, 128, TOKH], BF16).ap()
    d_hid = nc.dram_tensor("hidden", [8192, TOKH], BF16).ap()
    d_dnb = nc.dram_tensor("dnb", [8192, HID], BF16).ap()
    d_y1 = nc.dram_tensor("y1d", [TOKH, HID], F32).ap()
    _dumpable = {}

    ctx = contextlib.ExitStack()
    sb = lambda name, shape, dt: ctx.enter_context(nc.sbuf_tensor(name, shape, dt))

    identb = sb("s_identb", [128, 128], BF16)
    identf = sb("s_identf", [128, 128], F32)
    trisb = sb("s_trisb", [128, 128], BF16)
    triib = sb("s_triib", [128, 128], BF16)
    masksw = sb("s_masksw", [128, 128], BF16)
    maskiw = sb("s_maskiw", [128, 128], BF16)
    diagones = sb("s_diagones", [128, 128], BF16)
    hmask2 = sb("s_hmask2", [128, 1024], BF16)
    onesb = sb("s_onesb", [128, 128], BF16)
    ones1f = sb("s_ones1f", [128, 1], F32)
    onesrowf = sb("s_onesrowf", [1, 128], F32)
    onesrowb = sb("s_onesrowb", [1, 128], BF16)
    epscol = sb("s_epscol", [128, 1], F32)
    bacols = sb("s_bacols", [128, 8], F32)
    bbcol = sb("s_bbcol", [8, 1], F32)
    cwcols = sb("s_cwcols", [128, 96], F32)
    cbcols = sb("s_cbcols", [128, 24], F32)
    upsT = sb("s_upsT", [128, 64], F32)
    osccol = sb("s_osccol", [128, 1], F32)
    lnwrow = sb("s_lnwrow", [1, 128], F32)
    lnbrow = sb("s_lnbrow", [1, 128], F32)
    lnwbc = sb("lnwbc", [128, 128], F32)
    lnbbc = sb("lnbbc", [128, 128], F32)
    dnsxbc = sb("dnsxbc", [128, 2048], BF16)
    exoffs = sb("s_exoffs", [1, 2], I32)
    wstage = [sb(f"wstage{i}", [128, 1024], F32) for i in range(2)]
    pr1 = sb("pr1", [128, S], BF16)
    pr2 = sb("pr2", [128, S], F32)
    sqt = sb("sqt", [128, 512], F32)
    bsg = sb("bsg", [8, S], BF16)
    y1tile = sb("y1tile", [128, 2048], F32)
    r_mu = sb("r_mu", [1, 512], F32)
    r_m2 = sb("r_m2", [1, 512], F32)
    r_t1 = sb("r_t1", [1, 512], F32)
    r_t2 = sb("r_t2", [1, 512], F32)
    ktt = sb("ktt", [128, 128], BF16)
    ub = sb("ub", [128, 128], BF16)
    b0c = sb("b0c", [128, 128], BF16)
    bdk = sb("bdk", [128, 128], BF16)
    bdq = sb("bdq", [128, 128], BF16)
    gk8 = sb("gk8", [16, 128], BF16)
    gq8 = sb("gq8", [16, 128], BF16)
    c0s = sb("c0s", [128, 128], F32)
    q0s = sb("q0s", [128, 128], F32)
    pt_c = sb("pt_c", [128, 128], BF16)
    e1f = sb("e1f", [128, 128], F32)
    of32 = sb("of32", [128, 128], F32)
    osq = sb("osq", [128, 128], F32)
    onrm = sb("onrm", [128, 128], F32)
    col1 = sb("col1", [128, 1], F32)
    col2 = sb("col2", [128, 1], F32)
    col3 = sb("col3", [128, 1], F32)
    col4 = sb("col4", [128, 1], F32)
    st = sb("st", [128, 1024], BF16)
    e3bc = sb("e3bc", [128, 1024], BF16)
    t1s = sb("t1s", [128, 1024], BF16)
    epw = sb("epw", [128, 1024], BF16)
    m3 = sb("m3", [128, 1024], BF16)
    uesep = sb("uesep", [128, 1024], BF16)
    b0t = sb("b0t", [1, 2048], BF16)

    A1 = sb("arena1", [128, 65536], mybir.dt.uint8)
    A2 = sb("arena2", [128, 65536], mybir.dt.uint8)

    def view(arena, byte_off, cols, dt):
        nb = cols * mybir.dt.size(dt)
        return arena.ap()[:, byte_off:byte_off + nb].bitcast(dt)

    xn1 = view(A1, 0, 16 * 2048, BF16)
    wob = view(A1, 0, 16 * 2048, BF16)
    upwg = view(A1, 0, 16 * 512, BF16)
    hidq = view(A1, 0, 64 * 256, BF16)
    dnring = [view(A1, 32768 + i * 4096, 2048, BF16) for i in range(2)]
    wbf = view(A2, 0, 16 * 1024, BF16)
    mub = view(A2, 32768, 512, F32)
    rbb = view(A2, 34816, 512, F32)
    qs_t = view(A2, 0, HPC * TILE, BF16)
    ks_t = view(A2, 4096, HPC * TILE, BF16)
    kt_t = view(A2, 8192, HPC * TILE, BF16)
    za_t = view(A2, 12288, HPC * TILE, F32)
    g_t = view(A2, 20480, HPC * TILE, BF16)
    e1v_t = view(A2, 24576, CPT * 128, BF16)
    dp_t = view(A2, 28672, CPT * 128, BF16)
    e2_t = view(A2, 32768, CPT * 128, BF16)
    bbc_t = view(A2, 36864, HPC * TILE, BF16)
    vs_t = view(A2, 40960, HPC * TILE, BF16)
    og_t = view(A2, 45056, HPC * TILE, BF16)
    ogs = view(A2, 0, 16 * 1024, BF16)
    xn2 = view(A2, 32768, 16 * 1024, BF16)

    pctx = contextlib.ExitStack()
    psA = pctx.enter_context(nc.psum_tensor("psA", [128, 512], F32))
    psB = pctx.enter_context(nc.psum_tensor("psB", [128, 512], F32))
    psC = pctx.enter_context(nc.psum_tensor("psC", [128, 512], F32))
    psD = pctx.enter_context(nc.psum_tensor("psD", [128, 512], F32))
    psE = pctx.enter_context(nc.psum_tensor("psE", [128, 512], F32))
    psF = pctx.enter_context(nc.psum_tensor("psF", [128, 512], F32))
    psG = pctx.enter_context(nc.psum_tensor("psG", [128, 512], F32))
    psH = pctx.enter_context(nc.psum_tensor("psH", [128, 512], F32))
    BANKS4 = [psA, psB, psC, psD]
    psE_bf = psE.ap().bitcast(BF16)
    psG_bf = psG.ap().bitcast(BF16)

    sem_names = ["sdc", "sw0", "sw1", "sdt", "sdo", "sd", "gd",
                 "gp", "gb", "go", "gxa", "gxm", "gxp", "gy",
                 "t", "v", "c", "g", "cc"]
    sems = {n: ctx.enter_context(nc.semaphore(n)) for n in sem_names}

    _EPSAP[0] = epscol.ap()[:, 0:1]
    PLAN = {"sync": [], "tensor": [], "vector": [], "scalar": [], "gpsimd": []}
    CNT = {n: 0 for n in sem_names}

    _SELF = {"vector": "v", "scalar": "c"}

    def E(eng, fn, waits=(), inc=None, amt=1):
        w = [x for x in waits if x[1] > 0]
        s_self = _SELF.get(eng) or ("g" if (eng == "gpsimd" and inc == "g") else None)
        if s_self and CNT[s_self] > 0:
            w.append((s_self, CNT[s_self]))
        PLAN[eng].append((fn, w, inc, amt))
        if inc:
            CNT[inc] += amt
        return CNT[inc] if inc else None

    now = lambda s: CNT[s]

    def subap(b, col_off, dims):
        return AP(b.tensor, b.offset + col_off,
                  [list(b.ap[0])] + [list(d) for d in dims])

    def rep3(apx, col_off, blk, repn, width):
        return subap(apx, col_off, [[0, repn], [1, width]])

    def as3(apx, col_off, blk, n, width):
        return subap(apx, col_off, [[blk, n], [1, width]])

    for t_sb, nm in [(identb, "identb"), (identf, "identf"), (trisb, "trisb"),
                     (triib, "triib"), (masksw, "masksw"), (maskiw, "maskiw"),
                     (diagones, "diagones"), (hmask2, "hmask2"), (onesb, "onesb"),
                     (ones1f, "ones1f"), (onesrowf, "onesrowf"), (onesrowb, "onesrowb"),
                     (epscol, "epscol"), (bacols, "bacols"),
                     (bbcol, "bbcol"), (cwcols, "cwcols"), (cbcols, "cbcols"),
                     (upsT, "upsT"), (osccol, "osccol"), (lnwrow, "lnwrow"),
                     (lnbrow, "lnbrow"), (exoffs, "exoff")]:
        E("sync", lambda s, d=t_sb.ap()[:, :], src=d_in[nm]: s.dma_start(out=d, in_=src[:, :]),
          inc="sdc", amt=16)
    E("sync", lambda s, d=pr2.ap()[0:1, 0:2048], src=d_in["dnsrow"]: s.dma_start(
        out=d, in_=src[:, :]), inc="sdc", amt=16)
    CONSTS_SD = now("sdc")

    E("tensor", lambda t, o=psE.ap()[:, 0:128], l=onesrowf.ap()[:, :], r=lnwrow.ap()[:, :]:
      t.matmul(o, l, r, start=True, stop=True), waits=[("sdc", CONSTS_SD)], inc="t")
    E("tensor", lambda t, o=psE.ap()[:, 128:256], l=onesrowf.ap()[:, :], r=lnbrow.ap()[:, :]:
      t.matmul(o, l, r, start=True, stop=True), inc="t")
    TB = now("t")
    E("vector", lambda v, o=lnwbc.ap()[:, :], i=psE.ap()[:, 0:128]: v.tensor_copy(o, i),
      waits=[("t", TB)], inc="v")
    E("vector", lambda v, o=lnbbc.ap()[:, :], i=psE.ap()[:, 128:256]: v.tensor_copy(o, i),
      inc="v")
    for i in range(4):
        E("tensor", lambda t, o=psA.ap()[:, :], l=onesrowf.ap()[:, :],
          r=pr2.ap()[0:1, i * 512:(i + 1) * 512]: t.matmul(o, l, r, start=True, stop=True),
          waits=[("v", now("v"))], inc="t")
        E("vector", lambda v, o=dnsxbc.ap()[:, i * 512:(i + 1) * 512], ip=psA.ap()[:, :],
          sc=osccol.ap()[:, 0:1]: v.tensor_scalar(o, ip, sc, None, ALU.mult),
          waits=[("t", now("t"))], inc="v")

    sqt_rd_t = 0
    for tg in range(4):
        c0 = tg * 512
        bufdep = [(0, 0), (0, 0)]
        for d in range(16):
            buf = wstage[d % 2].ap()
            par = "sw%d" % (d % 2)
            dc, dt_ = bufdep[d % 2]
            E("sync", lambda s, o=buf[:, 0:512], src=d_in["xT"][d * 128:(d + 1) * 128, c0:c0 + 512]:
              s.dma_start(out=o, in_=src),
              waits=[("c", dc), ("t", dt_)] + ([("v", now("v"))] if d == 0 else []),
              inc=par, amt=16)
            ld = now(par)
            E("scalar", lambda sc, o=sqt.ap()[:, :], i=buf[:, 0:512]:
              sc.activation(o, i, AF.Square), waits=[(par, ld), ("t", sqt_rd_t)], inc="c")
            sq = now("c")
            E("tensor", lambda t, o=psA.ap()[0:1, :], l=ones1f.ap()[:, :], r=buf[:, 0:512],
              st=(d == 0), sp=(d == 15): t.matmul(o, l, r, start=st, stop=sp),
              waits=[(par, ld)] + ([("v", now("v"))] if d == 0 else []), inc="t")
            E("tensor", lambda t, o=psB.ap()[0:1, :], l=ones1f.ap()[:, :], r=sqt.ap()[:, :],
              st=(d == 0), sp=(d == 15): t.matmul(o, l, r, start=st, stop=sp),
              waits=[("c", sq)], inc="t")
            sqt_rd_t = now("t")
            bufdep[d % 2] = (sq, now("t"))
        TS = now("t")
        E("scalar", lambda sc, o=r_mu.ap()[:, :], i=psA.ap()[0:1, :]: sc.mul(o, i, 1.0 / 2048),
          waits=[("t", TS)], inc="c")
        E("scalar", lambda sc, o=r_m2.ap()[:, :], i=psB.ap()[0:1, :]: sc.mul(o, i, 1.0 / 2048),
          inc="c")
        E("scalar", lambda sc, o=r_t1.ap()[:, :], i=r_mu.ap()[:, :]: sc.activation(o, i, AF.Square),
          inc="c")
        E("vector", lambda v, o=r_t2.ap()[:, :], a=r_m2.ap()[:, :], b=r_t1.ap()[:, :]:
          v.tensor_sub(o, a, b), waits=[("c", now("c"))], inc="v")
        E("scalar", lambda sc, o=r_t1.ap()[:, :], i=r_t2.ap()[:, :]:
          sc.activation(o, i, AF.Ln, bias=epscol.ap()[0:1, 0:1]),
          waits=[("v", now("v"))], inc="c")
        E("scalar", lambda sc, o=r_t2.ap()[:, :], i=r_t1.ap()[:, :]:
          sc.activation(o, i, AF.Exp, scale=-0.5), inc="c")
        E("vector", lambda v: v.engine_nop(), waits=[("c", now("c"))], inc="v")
        E("tensor", lambda t, o=psA.ap()[:, :], l=onesrowf.ap()[:, :], r=r_mu.ap()[:, :]:
          t.matmul(o, l, r, start=True, stop=True), waits=[("v", now("v"))], inc="t")
        E("tensor", lambda t, o=psB.ap()[:, :], l=onesrowf.ap()[:, :], r=r_t2.ap()[:, :]:
          t.matmul(o, l, r, start=True, stop=True), inc="t")
        TS = now("t")
        E("vector", lambda v, o=mub, i=psA.ap()[:, :]: v.tensor_copy(o, i),
          waits=[("t", TS)], inc="v")
        E("vector", lambda v, o=rbb, i=psB.ap()[:, :]: v.tensor_copy(o, i), inc="v")
        MR = now("v")
        bufv = [MR, MR]
        for d in range(16):
            buf = wstage[d % 2].ap()
            par = "sw%d" % (d % 2)
            E("sync", lambda s, o=buf[:, 0:512], src=d_in["xT"][d * 128:(d + 1) * 128, c0:c0 + 512]:
              s.dma_start(out=o, in_=src), waits=[("v", bufv[d % 2])],
              inc=par, amt=16)
            ld = now(par)
            E("vector", lambda v, o=pr2.ap()[:, 0:512], a=buf[:, 0:512], b=mub:
              v.tensor_sub(o, a, b), waits=[(par, ld), ("v", MR)], inc="v")
            bufv[d % 2] = now("v")
            E("vector", lambda v, o=xn1[:, d * 2048 + c0: d * 2048 + c0 + 512],
              a=pr2.ap()[:, 0:512], b=rbb: v.tensor_mul(o, a, b), inc="v")
    XN1_V = now("v")

    d_stage = {"q": d_qs, "k": d_ks, "v": d_vs}
    for wname in ["q", "k", "v", "a", "g", "b"]:
        nch = 1 if wname == "b" else 8
        wcols = 8 if wname == "b" else 1024
        bufv = [now("v"), now("v")]
        for d in range(16):
            buf = wstage[d % 2].ap()
            par = "sw%d" % (d % 2)
            E("sync", lambda s, o=buf[:, 0:wcols],
              src=d_in["w" + wname][d * 128:(d + 1) * 128, 0:wcols]:
              s.dma_start(out=o, in_=src), waits=[("v", bufv[d % 2])],
              inc=par, amt=16)
            ld = now(par)
            E("vector", lambda v, o=wbf[:, d * 1024:d * 1024 + wcols], i=buf[:, 0:wcols]:
              v.tensor_copy(o, i),
              waits=[(par, ld)] + ([("t", now("t"))] if d == 0 else []), inc="v")
            bufv[d % 2] = now("v")
        WV = max(now("v"), XN1_V)
        for ch in range(nch):
            nrow = 8 if wname == "b" else 128
            for d in range(16):
                for tg in range(4):
                    E("tensor", lambda t, o=BANKS4[tg].ap()[0:nrow, :],
                      l=wbf[:, d * 1024 + ch * 128: d * 1024 + ch * 128 + nrow],
                      r=xn1[:, d * 2048 + tg * 512: d * 2048 + (tg + 1) * 512],
                      st=(d == 0), sp=(d == 15): t.matmul(o, l, r, start=st, stop=sp),
                      waits=([("v", WV)] if (d == 0 and tg == 0) else []), inc="t")
            TP = now("t")
            if wname in ("q", "k", "v"):
                wj = {"q": 0, "k": 1, "v": 2}[wname]
                for tg in range(4):
                    E("vector", lambda v, o=pr1.ap()[:, tg * 512:(tg + 1) * 512],
                      i=BANKS4[tg].ap()[:, :]: v.tensor_copy(o, i),
                      waits=[("t", TP)] + ([("gp", now("gp"))] if tg == 0 else []),
                      inc="v")
                E("vector", lambda v, o=pr2.ap()[:, :], i=pr1.ap()[:, :],
                  w=cwcols.ap()[:, wj * 32 + ch * 4 + 3: wj * 32 + ch * 4 + 4]:
                  v.tensor_scalar(o, i, w, None, ALU.mult),
                  waits=[("v", now("v"))], inc="v")
                for sft in range(1, 4):
                    E("vector", lambda v, o=pr2.ap()[:, sft:S], i=pr1.ap()[:, 0:S - sft],
                      w=cwcols.ap()[:, wj * 32 + ch * 4 + 3 - sft: wj * 32 + ch * 4 + 4 - sft],
                      acc=pr2.ap()[:, sft:S]:
                      v.scalar_tensor_tensor(o, i, w, acc, ALU.mult, ALU.add), inc="v")
                CV = now("v")
                E("scalar", lambda sc, o=pr1.ap()[:, :], i=pr2.ap()[:, :],
                  bia=cbcols.ap()[:, wj * 8 + ch: wj * 8 + ch + 1]:
                  sc.activation(o, i, AF.Silu, bias=bia), waits=[("v", CV)], inc="c")
                if wname == "k":
                    E("vector", lambda v, o=pr1.ap()[:, :], i=pr1.ap()[:, :]:
                      v.tensor_scalar_mul(o, i, float(DK) ** -0.5),
                      waits=[("c", now("c"))], inc="v")
                GW = [("c", now("c")), ("v", now("v"))]
                E("gpsimd", lambda g, o=d_stage[wname][ch * 128:(ch + 1) * 128, :],
                  i=pr1.ap()[:, :]: g.dma_start(out=o, in_=i), waits=GW, inc="gp", amt=16)
            elif wname == "a":
                for tg in range(4):
                    E("vector", lambda v, o=pr2.ap()[:, tg * 512:(tg + 1) * 512],
                      i=BANKS4[tg].ap()[:, :], bia=bacols.ap()[:, ch:ch + 1]:
                      v.tensor_scalar(o, i, bia, None, ALU.add),
                      waits=[("t", TP)] + ([("gp", now("gp"))] if tg == 0 else []),
                      inc="v")
                E("gpsimd", lambda g, o=d_zas[ch * 128:(ch + 1) * 128, :], i=pr2.ap()[:, :]:
                  g.dma_start(out=o, in_=i), waits=[("v", now("v"))], inc="gp", amt=16)
            elif wname == "g":
                for tg in range(4):
                    E("vector", lambda v, o=pr1.ap()[:, tg * 512:(tg + 1) * 512],
                      i=BANKS4[tg].ap()[:, :]: v.tensor_copy(o, i),
                      waits=[("t", TP)] + ([("gp", now("gp"))] if tg == 0 else []),
                      inc="v")
                E("gpsimd", lambda g, o=d_zgs[ch * 128:(ch + 1) * 128, :], i=pr1.ap()[:, :]:
                  g.dma_start(out=o, in_=i), waits=[("v", now("v"))], inc="gp", amt=16)
            else:
                for tg in range(4):
                    E("vector", lambda v, o=sqt.ap()[0:8, :], i=BANKS4[tg].ap()[0:8, :],
                      bia=bbcol.ap()[:, 0:1]: v.tensor_scalar(o, i, bia, None, ALU.add),
                      waits=[("t", TP), ("c", now("c"))], inc="v")
                    E("scalar", lambda sc, o=bsg.ap()[:, tg * 512:(tg + 1) * 512],
                      i=sqt.ap()[0:8, :]: sc.activation(o, i, AF.Sigmoid),
                      waits=[("v", now("v"))], inc="c")
            gv, gc = now("v"), now("c")
            E("tensor", lambda t: t.nop(), waits=[("v", gv), ("c", gc)], inc="t")

    PROJ_GP = now("gp")
    PROJ_C = now("c")

    E("vector", lambda v, o=st.ap()[:, :]: v.memset(o, 0.0), inc="v")

    NT = min(int(os.environ.get("KTILES", str(NTILE))), NTILE)
    for mt in range(NT if PHASE >= 2 else 0):
        tc0 = mt * TILE
        half_idx, inner0 = mt // 4, (mt % 4) * TILE
        ldw = [("gp", PROJ_GP), ("c", PROJ_C), ("v", now("v")), ("t", now("t")),
               ("g", now("g"))]
        def cmdst(tens, h):
            return subap(tens, h * 16, [[128, 16], [1, 16]])
        for h in range(HPC):
            for dst, src in [(qs_t, d_qs), (ks_t, d_ks), (vs_t, d_vs)]:
                E("sync", lambda s, o=cmdst(dst, h),
                  i=src[h * 128:(h + 1) * 128, tc0:tc0 + TILE].rearrange(
                      "p (c t) -> p c t", t=16):
                  s.dma_start(out=o, in_=i), waits=ldw if h == 0 else [], inc="sdt", amt=16)
            E("sync", lambda s, o=cmdst(za_t, h),
              i=d_zas[h * 128:(h + 1) * 128, tc0:tc0 + TILE].rearrange(
                  "p (c t) -> p c t", t=16):
              s.dma_start(out=o, in_=i), inc="sdt", amt=16)
            E("sync", lambda s, o=cmdst(g_t, h),
              i=d_zgs[h * 128:(h + 1) * 128, tc0:tc0 + TILE].rearrange(
                  "p (c t) -> p c t", t=16):
              s.dma_start(out=o, in_=i), inc="sdt", amt=16)
        TL = now("sdt")
        for h in range(HPC):
            E("gpsimd", lambda g,
              o=subap(b0t.ap()[:, :], h * 16, [[128, 16], [1, 16]]),
              i=bsg.ap()[h:h + 1, tc0:tc0 + TILE].rearrange("p (c t) -> p c t", t=16):
              g.dma_start(out=o, in_=i),
              waits=[("c", PROJ_C), ("t", now("t"))] if h == 0 else [], inc="gb", amt=16)
        BG = now("gb")
        E("scalar", lambda sc, o=g_t[:, :], i=g_t[:, :]: sc.activation(o, i, AF.Sigmoid),
          waits=[("sdt", TL)], inc="c")
        GSIG = now("c")
        for i in range(4):
            E("tensor", lambda t, o=psA.ap()[:, :], l=onesrowb.ap()[:, :],
              r=b0t.ap()[:, i * 512:(i + 1) * 512]: t.matmul(o, l, r, start=True, stop=True),
              waits=[("gb", BG), ("v", now("v"))], inc="t")
            E("vector", lambda v, o=bbc_t[:, i * 512:(i + 1) * 512], ip=psA.ap()[:, :]:
              v.tensor_copy(o, ip), waits=[("t", now("t"))], inc="v")
        E("vector", lambda v, o=kt_t[:, :], a=ks_t[:, :], b=bbc_t[:, :]:
          v.tensor_mul(o, a, b), waits=[("sdt", TL), ("v", now("v")), ("go", now("go"))],
          inc="v")
        KTV = now("v")

        for ci in range(CPT):
            cb2 = ci * 128
            za_sl = za_t[:, cb2:cb2 + 128]
            vs_sl = vs_t[:, cb2:cb2 + 128]
            kt_sl3 = kt_t[:, cb2:cb2 + 128]
            g_sl2 = g_t[:, cb2:cb2 + 128]
            E("tensor", lambda t, o=psG.ap()[:, 0:128], i=za_sl, idn=identf.ap()[:, :]:
              t.transpose(o, i, idn),
              waits=[("v", now("v")), ("c", now("c")), ("g", now("g"))], inc="t")
            TZ = now("t")
            E("scalar", lambda sc, o=e1f.ap()[:, :], i=psG.ap()[:, 0:128]:
              sc.activation(o, i, AF.Exp, scale=-1.0), waits=[("t", TZ)], inc="c")
            PEX = now("c")
            E("vector", lambda v, o=osq.ap()[:, :], i=e1f.ap()[:, :]:
              v.tensor_scalar(o, i, 1.0, None, ALU.add), waits=[("c", PEX)], inc="v")
            TP1V = now("v")
            E("scalar", lambda sc, o=pt_c.ap()[:, :], i=osq.ap()[:, :]:
              sc.activation(o, i, AF.Ln), waits=[("v", TP1V)], inc="c")
            PE_ = now("c")
            E("vector", lambda v, o=dp_t[:, ci * 128:(ci + 1) * 128], i=osq.ap()[:, :]:
              v.tensor_scalar(o, i, -1.0, None, ALU.mult), inc="v")
            DPV = now("v")
            E("tensor", lambda t, o=psE_bf[:, 512:640], i=vs_sl, idn=identb.ap()[:, :]:
              t.transpose(o, i, idn), waits=[("v", DPV)], inc="t")
            E("tensor", lambda t, o=psF.ap()[:, 128:256], l=trisb.ap()[:, :], r=pt_c.ap()[:, :]:
              t.matmul(o, l, r, start=True, stop=True), waits=[("c", PE_)], inc="t")
            E("tensor", lambda t, o=psF.ap()[:, 256:384], l=triib.ap()[:, :], r=pt_c.ap()[:, :]:
              t.matmul(o, l, r, start=True, stop=True), inc="t")
            TCUM = now("t")
            E("scalar", lambda sc, o=e1f.ap()[:, :], i=psF.ap()[:, 128:256]:
              sc.activation(o, i, AF.Exp), waits=[("t", TCUM), ("v", DPV)], inc="c")
            E("scalar", lambda sc, o=e2_t[:, ci * 128:(ci + 1) * 128], i=psF.ap()[:, 256:384]:
              sc.activation(o, i, AF.Exp, scale=-1.0), inc="c")
            E12 = now("c")
            E("vector", lambda v, o=e1v_t[:, ci * 128:(ci + 1) * 128], a=e1f.ap()[:, :],
              b=psE_bf[:, 512:640]: v.tensor_mul(o, a, b),
              waits=[("c", E12), ("t", now("t"))], inc="v")
            E1VV = now("v")
            E("tensor", lambda t, o=psE_bf[:, 640:768], i=kt_sl3, idn=identb.ap()[:, :]:
              t.transpose(o, i, idn), waits=[("v", now("v"))], inc="t")
            TK = now("t")
            E("vector", lambda v, o=ktt.ap()[:, :], i=psE_bf[:, 640:768]:
              v.tensor_copy(o, i), waits=[("t", TK)], inc="v")
            KTTV = now("v")
            for h in range(HPC):
                q_sl = qs_t[:, cb2 + h * 16: cb2 + h * 16 + 16]
                k_sl = ks_t[:, cb2 + h * 16: cb2 + h * 16 + 16]
                ktl = kt_t[:, cb2 + h * 16: cb2 + h * 16 + 16]
                E("tensor", lambda t, o=psG.ap()[0:16, 128 + h * 16:128 + h * 16 + 16],
                  l=k_sl, r=ktl: t.matmul(o, l, r, start=True, stop=True),
                  waits=[("c", E12), ("v", KTTV)] if h == 0 else [], inc="t")
                E("tensor", lambda t, o=psG.ap()[0:16, 256 + h * 16:256 + h * 16 + 16],
                  l=q_sl, r=ktl: t.matmul(o, l, r, start=True, stop=True), inc="t")
            TG = now("t")
            E("scalar", lambda sc, o=gk8.ap()[:, :], i=psG.ap()[0:16, 128:256]:
              sc.copy(o, i), waits=[("t", TG)], inc="c")
            E("scalar", lambda sc, o=gq8.ap()[:, :], i=psG.ap()[0:16, 256:384]:
              sc.copy(o, i), inc="c")
            GE = now("c")
            E("tensor", lambda t, o=psG_bf[:, 768:784], i=gk8.ap()[:, :],
              idn=identb.ap()[0:16, 0:16]: t.transpose(o, i, idn),
              waits=[("c", GE)], inc="t")
            E("tensor", lambda t, o=psG_bf[:, 784:800], i=gq8.ap()[:, :],
              idn=identb.ap()[0:16, 0:16]: t.transpose(o, i, idn), inc="t")
            TGT = now("t")
            E("vector", lambda v, o=as3(bdk.ap()[:, :], 0, 16, 8, 16),
              a=rep3(psG_bf, 768, 0, 8, 16),
              b=as3(masksw.ap()[:, :], 0, 16, 8, 16): v.tensor_tensor(o, a, b, ALU.mult),
              waits=[("t", TGT)], inc="v")
            E("vector", lambda v, o=as3(bdq.ap()[:, :], 0, 16, 8, 16),
              a=rep3(psG_bf, 784, 0, 8, 16),
              b=as3(maskiw.ap()[:, :], 0, 16, 8, 16): v.tensor_tensor(o, a, b, ALU.mult),
              inc="v")
            BDV = now("v")
            for h in range(HPC):
                q_sl = qs_t[:, cb2 + h * 16: cb2 + h * 16 + 16]
                k_sl = ks_t[:, cb2 + h * 16: cb2 + h * 16 + 16]
                E("tensor", lambda t, o=psF.ap()[:, 128 + h * 16:128 + h * 16 + 16],
                  l=st.ap()[:, h * 128:(h + 1) * 128], r=k_sl:
                  t.matmul(o, l, r, start=True, stop=True),
                  waits=[("v", BDV), ("c", GE)] if h == 0 else [], inc="t")
                E("tensor", lambda t, o=psF.ap()[:, 256 + h * 16:256 + h * 16 + 16],
                  l=st.ap()[:, h * 128:(h + 1) * 128], r=q_sl:
                  t.matmul(o, l, r, start=True, stop=True), inc="t")
            TCQ = now("t")
            E("vector", lambda v, o=c0s.ap()[:, :], i=psF.ap()[:, 128:256]:
              v.tensor_copy(o, i), waits=[("t", TCQ)], inc="v")
            E("vector", lambda v, o=q0s.ap()[:, :], i=psF.ap()[:, 256:384]:
              v.tensor_copy(o, i), inc="v")
            CQE = now("v")
            E("tensor", lambda t, o=psE.ap()[:, 128:256], i=c0s.ap()[:, :],
              idn=identf.ap()[:, :]: t.transpose(o, i, idn), waits=[("v", CQE)], inc="t")
            E("tensor", lambda t, o=psF.ap()[:, 0:128], i=q0s.ap()[:, :],
              idn=identf.ap()[:, :]: t.transpose(o, i, idn), inc="t")
            TCT = now("t")
            E("vector", lambda v, o=b0c.ap()[:, :], a=psE.ap()[:, 128:256],
              b=e1v_t[:, ci * 128:(ci + 1) * 128]: v.tensor_sub(o, a, b),
              waits=[("t", TCT), ("v", E1VV)], inc="v")
            E("vector", lambda v, o=ub.ap()[:, :], a=b0c.ap()[:, :],
              b=dp_t[:, ci * 128:(ci + 1) * 128]:
              v.tensor_mul(o, a, b), inc="v")
            for ts_ in range(1, 16):
                E("tensor", lambda t, o=psE.ap()[:, 0:128], l=bdk.ap()[:, :], r=ub.ap()[:, :]:
                  t.matmul(o, l, r, start=True, stop=False),
                  waits=[("v", now("v"))], inc="t")
                E("tensor", lambda t, o=psE.ap()[:, 0:128], l=identb.ap()[:, :],
                  r=b0c.ap()[:, :]: t.matmul(o, l, r, start=False, stop=True),
                  inc="t")
                E("vector", lambda v, o=ub.ap()[:, :], a=psE.ap()[:, 0:128],
                  b=dp_t[:, ci * 128:(ci + 1) * 128]:
                  v.tensor_mul(o, a, b),
                  waits=[("t", now("t"))], inc="v")
            SOLV = now("v")
            E("tensor", lambda t, o=psF.ap()[:, 0:128], l=bdq.ap()[:, :], r=ub.ap()[:, :]:
              t.matmul(o, l, r, start=False, stop=True), waits=[("v", SOLV)], inc="t")
            TO = now("t")
            E("vector", lambda v, o=of32.ap()[:, :], a=psF.ap()[:, 0:128],
              b=e2_t[:, ci * 128:(ci + 1) * 128]: v.tensor_mul(o, a, b),
              waits=[("t", TO)], inc="v")
            OV = now("v")
            E("vector", lambda v, o=col1.ap()[:, :], i=of32.ap()[:, :]:
              v.reduce_sum(o, i, AXX), inc="v")
            E("scalar", lambda sc, o=osq.ap()[:, :], i=of32.ap()[:, :]:
              sc.activation(o, i, AF.Square), waits=[("v", OV)], inc="c")
            E("vector", lambda v, o=col2.ap()[:, :], i=osq.ap()[:, :]:
              v.reduce_sum(o, i, AXX), waits=[("c", now("c"))], inc="v")
            E("scalar", lambda sc, o=col1.ap()[:, :], i=col1.ap()[:, :]:
              sc.mul(o, i, 1.0 / 128), waits=[("v", now("v"))], inc="c")
            E("scalar", lambda sc, o=col2.ap()[:, :], i=col2.ap()[:, :]:
              sc.mul(o, i, 1.0 / 128), inc="c")
            E("scalar", lambda sc, o=col3.ap()[:, :], i=col1.ap()[:, :]:
              sc.activation(o, i, AF.Square), inc="c")
            E("vector", lambda v, o=col4.ap()[:, :], a=col2.ap()[:, :], b=col3.ap()[:, :]:
              v.tensor_sub(o, a, b), waits=[("c", now("c"))], inc="v")
            E("scalar", lambda sc, o=col3.ap()[:, :], i=col4.ap()[:, :],
              b_=epscol.ap()[:, 0:1]:
              sc.activation(o, i, AF.Ln, bias=b_), waits=[("v", now("v"))], inc="c")
            E("scalar", lambda sc, o=col4.ap()[:, :], i=col3.ap()[:, :]:
              sc.activation(o, i, AF.Exp, scale=-0.5), inc="c")
            E("vector", lambda v: v.engine_nop(), waits=[("c", now("c"))], inc="v")
            E("vector", lambda v, o=of32.ap()[:, :], i=of32.ap()[:, :],
              s1=col1.ap()[:, 0:1], s2=col4.ap()[:, 0:1]:
              v.tensor_scalar(o, i, s1, s2, ALU.subtract, ALU.mult), inc="v")
            E("vector", lambda v, o=of32.ap()[:, :], a=of32.ap()[:, :], b=lnwbc.ap()[:, :]:
              v.tensor_mul(o, a, b), inc="v")
            E("vector", lambda v, o=onrm.ap()[:, :], a=of32.ap()[:, :], b=lnbbc.ap()[:, :]:
              v.tensor_add(o, a, b), inc="v")
            ONV = now("v")
            E("tensor", lambda t, o=psE.ap()[:, 384:512], i=onrm.ap()[:, :],
              idn=identf.ap()[:, :]: t.transpose(o, i, idn), waits=[("v", ONV)], inc="t")
            TON = now("t")
            E("vector", lambda v, o=og_t[:, cb2:cb2 + 128], a=psE.ap()[:, 384:512],
              b=g_sl2: v.tensor_tensor(o, a, b, ALU.mult),
              waits=[("t", TON), ("c", GSIG)], inc="v")
            E("vector", lambda v, o=as3(epw.ap()[:, :], 0, 128, 8, 128),
              a=rep3(pt_c.ap()[:, :], 0, 0, 8, 128),
              b=as3(hmask2.ap()[:, :], 0, 128, 8, 128): v.tensor_tensor(o, a, b, ALU.mult),
              waits=[("c", now("c"))], inc="v")
            EPV = now("v")
            E("tensor", lambda t, o=psG.ap()[:, :], l=onesb.ap()[:, :], r=epw.ap()[:, 0:512]:
              t.matmul(o, l, r, start=True, stop=True), waits=[("v", EPV)], inc="t")
            E("tensor", lambda t, o=psH.ap()[:, :], l=onesb.ap()[:, :], r=epw.ap()[:, 512:1024]:
              t.matmul(o, l, r, start=True, stop=True), inc="t")
            TE3 = now("t")
            E("scalar", lambda sc, o=e3bc.ap()[:, 0:512], i=psG.ap()[:, :]:
              sc.activation(o, i, AF.Exp, scale=-1.0), waits=[("t", TE3)], inc="c")
            E("scalar", lambda sc, o=e3bc.ap()[:, 512:1024], i=psH.ap()[:, :]:
              sc.activation(o, i, AF.Exp, scale=-1.0), inc="c")
            E3V = now("c")
            E("gpsimd", lambda g, o=m3.ap()[:, :], a=e3bc.ap()[:, :], b=hmask2.ap()[:, :]:
              g.tensor_mul(o, a, b), waits=[("c", E3V)], inc="g")
            E("gpsimd", lambda g, o=t1s.ap()[:, :], a=st.ap()[:, :], b=e3bc.ap()[:, :]:
              g.tensor_mul(o, a, b), inc="g")
            E("gpsimd", lambda g, o=as3(uesep.ap()[:, :], 0, 128, 8, 128),
              a=rep3(ub.ap()[:, :], 0, 0, 8, 128),
              b=as3(m3.ap()[:, :], 0, 128, 8, 128): g.tensor_tensor(o, a, b, ALU.mult),
              waits=[("v", SOLV)], inc="g")
            GS = now("g")
            for i in range(2):
                bank = psG if i == 0 else psH
                E("tensor", lambda t, o=bank.ap()[:, :], l=identb.ap()[:, :],
                  r=t1s.ap()[:, i * 512:(i + 1) * 512]:
                  t.matmul(o, l, r, start=True, stop=False),
                  waits=[("g", GS), ("c", E3V)] if i == 0 else [], inc="t")
                E("tensor", lambda t, o=bank.ap()[:, :], l=ktt.ap()[:, :],
                  r=uesep.ap()[:, i * 512:(i + 1) * 512]:
                  t.matmul(o, l, r, start=False, stop=True), inc="t")
            TST = now("t")
            E("vector", lambda v, o=st.ap()[:, 0:512], i=psG.ap()[:, :]:
              v.tensor_copy(o, i), waits=[("t", TST)], inc="v")
            E("vector", lambda v, o=st.ap()[:, 512:1024], i=psH.ap()[:, :]:
              v.tensor_copy(o, i), inc="v")
            gv, gc, gg = now("v"), now("c"), now("g")
            E("tensor", lambda t: t.nop(), waits=[("v", gv), ("c", gc), ("g", gg)], inc="t")

        OGV = now("v")
        for h in range(HPC):
            E("gpsimd", lambda g,
              o=d_ogl[half_idx, h, :, inner0:inner0 + TILE].rearrange(
                  "p (c t) -> p c t", t=16),
              i=subap(og_t, h * 16, [[128, 16], [1, 16]]):
              g.dma_start(out=o, in_=i),
              waits=[("v", OGV)] if h == 0 else [], inc="go", amt=16)

    DELTA_GO = now("go")

    ogl_flat = d_ogl.rearrange("a b c d -> (a b c d)")
    agout_flat = d_agout.rearrange("a b c d -> (a b c d)")
    agin_flat = d_agin.rearrange("a b c -> (a b c)")
    ogf_flat = d_ogf.rearrange("a b c -> (a b c)")

    def do_agin(g):
        off_b = g.value_load(exoffs.ap()[0:1, 1:2])
        return g.dma_start(out=agin_flat, in_=AP(ogl_flat.tensor, off_b * 1, [[1, BLKE]]))

    def do_mine(g):
        off_a = g.value_load(exoffs.ap()[0:1, 0:1])
        return g.dma_start(out=AP(ogf_flat.tensor, off_a * 1, [[1, BLKE]]),
                           in_=AP(ogl_flat.tensor, off_a * 1, [[1, BLKE]]))

    def do_partner(g):
        off_b = g.value_load(exoffs.ap()[0:1, 1:2])
        return g.dma_start(out=AP(ogf_flat.tensor, off_b * 1, [[1, BLKE]]),
                           in_=AP(agout_flat.tensor, off_b * 1, [[1, BLKE]]))

    if PHASE >= 3:
        E("gpsimd", do_agin, waits=[("go", DELTA_GO)], inc="gxa", amt=16)
        E("gpsimd", lambda g: g.collective_compute(
            "AllGather", ALU.bypass, replica_groups=[[0, 1], [2, 3], [4, 5], [6, 7]],
            ins=[d_agin.opt()], outs=[d_agout.opt()]),
            waits=[("gxa", now("gxa"))], inc="cc")
        E("gpsimd", do_mine, waits=[("go", DELTA_GO)], inc="gxm", amt=16)
        E("gpsimd", do_partner, waits=[("cc", now("cc"))], inc="gxp", amt=16)
    OGF_W = [("gxm", now("gxm")), ("gxp", now("gxp"))]

    wovdep = 0
    for c in range(16 if PHASE >= 4 else 0):
        E("sync", lambda s, o=wstage[0].ap()[:, :], i=d_in["wo"][c * 128:(c + 1) * 128, 0:1024]:
          s.dma_start(out=o, in_=i), waits=[("v", wovdep)], inc="sw0", amt=16)
        E("sync", lambda s, o=wstage[1].ap()[:, :], i=d_in["wo"][c * 128:(c + 1) * 128, 1024:2048]:
          s.dma_start(out=o, in_=i), inc="sw1", amt=16)
        ld0, ld1 = now("sw0"), now("sw1")
        E("vector", lambda v, o=wob[:, c * 2048:c * 2048 + 1024], i=wstage[0].ap()[:, :]:
          v.tensor_copy(o, i),
          waits=[("sw0", ld0)] + ([("t", now("t"))] if c == 0 else []), inc="v")
        E("vector", lambda v, o=wob[:, c * 2048 + 1024:(c + 1) * 2048], i=wstage[1].ap()[:, :]:
          v.tensor_copy(o, i), waits=[("sw1", ld1)], inc="v")
        wovdep = now("v")
    WOBV = now("v")
    for c in range(16 if PHASE >= 4 else 0):
        E("sync", lambda s, o=ogs[:, c * 1024:(c + 1) * 1024], i=d_ogf[c, :, :]:
          s.dma_start(out=o, in_=i), waits=OGF_W if c == 0 else [],
          inc="sdo", amt=16)
    OGSD = now("sdo")

    for tt in range(8 if PHASE >= 4 else 0):
        t0 = tt * 128
        for c in range(16):
            for dgi in range(4):
                E("tensor", lambda t, o=BANKS4[dgi].ap()[:, :],
                  l=ogs[:, c * 1024 + t0: c * 1024 + t0 + 128],
                  r=wob[:, c * 2048 + dgi * 512: c * 2048 + (dgi + 1) * 512],
                  st=(c == 0), sp=(c == 15): t.matmul(o, l, r, start=st, stop=sp),
                  waits=[("sdo", OGSD), ("v", WOBV)] if (tt == 0 and c == 0 and dgi == 0) else
                        ([("v", now("v"))] if (c == 0 and dgi == 0) else []),
                  inc="t")
        TWO_ = now("t")
        for bi in range(4):
            E("vector", lambda v, o=pr2.ap()[:, bi * 512:(bi + 1) * 512],
              i=BANKS4[bi].ap()[:, :]: v.tensor_copy(o, i),
              waits=[("t", TWO_)] if bi == 0 else [], inc="v")
        AV = now("v")
        E("vector", lambda v, o=col1.ap()[:, :], i=pr2.ap()[:, :]: v.reduce_sum(o, i, AXX),
          inc="v")
        E("vector", lambda v, o=col3.ap()[:, :]: v.memset(o, 0.0), inc="v")
        for pi in range(4):
            E("scalar", lambda sc, o=sqt.ap()[:, :], i=pr2.ap()[:, pi * 512:(pi + 1) * 512]:
              sc.activation(o, i, AF.Square), waits=[("v", now("v"))], inc="c")
            E("vector", lambda v, o=col2.ap()[:, :], i=sqt.ap()[:, :]:
              v.reduce_sum(o, i, AXX), waits=[("c", now("c"))], inc="v")
            E("vector", lambda v, o=col3.ap()[:, :], a=col3.ap()[:, :], b=col2.ap()[:, :]:
              v.tensor_add(o, a, b), inc="v")
        _ln_cols(E, now, col1, col3, col2, col4, 2048.0)
        E("sync", lambda s, o=wstage[0].ap()[:, :], i=d_in["xres"][t0:t0 + 128, 0:1024]:
          s.dma_start(out=o, in_=i), waits=[("v", now("v"))], inc="sw0", amt=16)
        E("sync", lambda s, o=wstage[1].ap()[:, :], i=d_in["xres"][t0:t0 + 128, 1024:2048]:
          s.dma_start(out=o, in_=i), inc="sw1", amt=16)
        XRD = [("sw0", now("sw0")), ("sw1", now("sw1"))]
        E("vector", lambda v, o=y1tile.ap()[:, :], i=pr2.ap()[:, :],
          s1=col1.ap()[:, 0:1], s2=col4.ap()[:, 0:1]:
          v.tensor_scalar(o, i, s1, s2, ALU.subtract, ALU.mult),
          waits=XRD + [("v", now("v")), ("gy", now("gy"))], inc="v")
        E("vector", lambda v, o=y1tile.ap()[:, 0:1024], a=y1tile.ap()[:, 0:1024],
          b=wstage[0].ap()[:, :]: v.tensor_add(o, a, b), inc="v")
        E("vector", lambda v, o=y1tile.ap()[:, 1024:2048], a=y1tile.ap()[:, 1024:2048],
          b=wstage[1].ap()[:, :]: v.tensor_add(o, a, b), inc="v")
        Y1V = now("v")
        E("gpsimd", lambda g, o=d_y1[t0:t0 + 128, :], i=y1tile.ap()[:, :]:
          g.dma_start(out=o, in_=i), waits=[("v", Y1V)], inc="gy", amt=16)
        E("vector", lambda v, o=col1.ap()[:, :], i=y1tile.ap()[:, :]:
          v.reduce_sum(o, i, AXX), inc="v")
        E("vector", lambda v, o=col3.ap()[:, :]: v.memset(o, 0.0), inc="v")
        for pi in range(4):
            E("scalar", lambda sc, o=sqt.ap()[:, :], i=y1tile.ap()[:, pi * 512:(pi + 1) * 512]:
              sc.activation(o, i, AF.Square), waits=[("v", now("v"))], inc="c")
            E("vector", lambda v, o=col2.ap()[:, :], i=sqt.ap()[:, :]:
              v.reduce_sum(o, i, AXX), waits=[("c", now("c"))], inc="v")
            E("vector", lambda v, o=col3.ap()[:, :], a=col3.ap()[:, :], b=col2.ap()[:, :]:
              v.tensor_add(o, a, b), inc="v")
        _ln_cols(E, now, col1, col3, col2, col4, 2048.0)
        E("vector", lambda v, o=pr1.ap()[:, :], i=y1tile.ap()[:, :],
          s1=col1.ap()[:, 0:1], s2=col4.ap()[:, 0:1]:
          v.tensor_scalar(o, i, s1, s2, ALU.subtract, ALU.mult),
          waits=[("t", now("t"))], inc="v")
        X2V = now("v")
        for c in range(16):
            E("tensor", lambda t, o=psE_bf[:, 0:128], i=pr1.ap()[:, c * 128:(c + 1) * 128],
              idn=identb.ap()[:, :]: t.transpose(o, i, idn),
              waits=[("v", X2V if c == 0 else now("v"))], inc="t")
            E("vector", lambda v, o=xn2[:, c * 1024 + t0: c * 1024 + t0 + 128],
              i=psE_bf[:, 0:128]: v.tensor_copy(o, i), waits=[("t", now("t"))], inc="v")
        gv = now("v")
        E("tensor", lambda t: t.nop(), waits=[("v", gv)], inc="t")
    XN2V = now("v")

    dncdep, dngdep = 0, now("gd")
    for r in range(64 if PHASE >= 5 else 0):
        E("sync", lambda s, o=wstage[0].ap()[:, :], i=d_in["dnwT"][r * 128:(r + 1) * 128, 0:1024]:
          s.dma_start(out=o, in_=i),
          waits=[("c", dncdep)] + ([("v", now("v"))] if r == 0 else []), inc="sd", amt=16)
        E("sync", lambda s, o=wstage[1].ap()[:, :], i=d_in["dnwT"][r * 128:(r + 1) * 128, 1024:2048]:
          s.dma_start(out=o, in_=i), inc="sd", amt=16)
        ld = now("sd")
        E("scalar", lambda sc, o=pr1.ap()[:, 0:1024], i=wstage[0].ap()[:, :]:
          sc.activation(o, i, AF.Sign),
          waits=[("sd", ld), ("gd", dngdep)] + ([("t", now("t"))] if r == 0 else []),
          inc="c")
        E("scalar", lambda sc, o=pr1.ap()[:, 1024:2048], i=wstage[1].ap()[:, :]:
          sc.activation(o, i, AF.Sign), inc="c")
        sg = now("c")
        dncdep = sg
        E("gpsimd", lambda g, o=d_dnb[r * 128:(r + 1) * 128, :], i=pr1.ap()[:, 0:2048]:
          g.dma_start(out=o, in_=i), waits=[("c", sg)], inc="gd", amt=16)
        dngdep = now("gd")
    DNB_GD = now("gd")

    upcdep = [now("c"), now("c")]
    for hg in range(16 if PHASE >= 5 else 0):
        h0 = hg * 512
        for d in range(16):
            buf = wstage[d % 2].ap()
            E("sync", lambda s, o=buf[:, 0:512],
              i=d_in["upwT"][d * 128:(d + 1) * 128, h0:h0 + 512]:
              s.dma_start(out=o, in_=i), waits=[("c", upcdep[d % 2])],
              inc="sd", amt=16)
            ld = now("sd")
            E("scalar", lambda sc, o=upwg[:, d * 512:(d + 1) * 512], i=buf[:, 0:512]:
              sc.activation(o, i, AF.Sign),
              waits=[("sd", ld)] + ([("t", now("t"))] if d == 0 else []), inc="c")
            upcdep[d % 2] = now("c")
        UPC = now("c")
        for hc in range(4):
            for d in range(16):
                for tg in range(2):
                    E("tensor", lambda t, o=(psA if tg == 0 else psB).ap()[:, :],
                      l=upwg[:, d * 512 + hc * 128: d * 512 + (hc + 1) * 128],
                      r=xn2[:, d * 1024 + tg * 512: d * 1024 + (tg + 1) * 512],
                      st=(d == 0), sp=(d == 15): t.matmul(o, l, r, start=st, stop=sp),
                      waits=[("c", UPC), ("v", XN2V)] if (hc == 0 and d == 0 and tg == 0)
                      else ([("c", now("c"))] if (d == 0 and tg == 0) else []),
                      inc="t")
            TU = now("t")
            for tg in range(2):
                E("scalar", lambda sc, o=pr1.ap()[:, tg * 512:(tg + 1) * 512],
                  i=(psA if tg == 0 else psB).ap()[:, :],
                  s_=upsT.ap()[:, hg * 4 + hc: hg * 4 + hc + 1]:
                  sc.activation(o, i, AF.Relu, scale=s_),
                  waits=[("t", TU)] + ([("gd", now("gd"))] if tg == 0 else []), inc="c")
            HCC = now("c")
            E("gpsimd", lambda g, o=d_hid[h0 + hc * 128: h0 + (hc + 1) * 128, :],
              i=pr1.ap()[:, 0:1024]: g.dma_start(out=o, in_=i),
              waits=[("c", HCC)], inc="gd", amt=16)
    HID_GD = now("gd")

    BANKS8 = [psA, psB, psC, psD, psE, psF, psG, psH]
    for tq in range(4 if PHASE >= 6 else 0):
        q0 = tq * 256
        for r in range(64):
            E("sync", lambda s, o=hidq[:, r * 256:(r + 1) * 256],
              i=d_hid[r * 128:(r + 1) * 128, q0:q0 + 256]:
              s.dma_start(out=o, in_=i),
              waits=[("gd", HID_GD), ("t", now("t"))] if r == 0 else [], inc="sd", amt=16)
        HQ = now("sd")
        ring_t = [0, 0]
        for r in range(64):
            rb = r % 2
            E("sync", lambda s, o=dnring[rb], i=d_dnb[r * 128:(r + 1) * 128, :]:
              s.dma_start(out=o, in_=i),
              waits=[("gd", DNB_GD), ("sd", HQ if r == 0 else 0), ("t", ring_t[rb])],
              inc="sd", amt=16)
            ld = now("sd")
            for ttl in range(2):
                for dgi in range(4):
                    bank = BANKS8[ttl * 4 + dgi]
                    E("tensor", lambda t, o=bank.ap()[:, :],
                      l=hidq[:, r * 256 + ttl * 128: r * 256 + (ttl + 1) * 128],
                      r_=dnring[rb][:, dgi * 512:(dgi + 1) * 512],
                      st=(r == 0), sp=(r == 63): t.matmul(o, l, r_, start=st, stop=sp),
                      waits=[("sd", ld), ("v", now("v"))] if (ttl == 0 and dgi == 0) else [],
                      inc="t")
            ring_t[rb] = now("t")
        TD = now("t")
        for ttl in range(2):
            t0 = q0 + ttl * 128
            for bi in range(4):
                bank = BANKS8[ttl * 4 + bi]
                E("vector", lambda v, o=pr2.ap()[:, bi * 512:(bi + 1) * 512],
                  a=bank.ap()[:, :], b=dnsxbc.ap()[:, bi * 512:(bi + 1) * 512]:
                  v.tensor_tensor(o, a, b, ALU.mult),
                  waits=[("t", TD), ("gd", now("gd"))] if bi == 0 else [], inc="v")
            FV = now("v")
            E("vector", lambda v, o=col1.ap()[:, :], i=pr2.ap()[:, :]:
              v.reduce_sum(o, i, AXX), inc="v")
            E("vector", lambda v, o=col3.ap()[:, :]: v.memset(o, 0.0), inc="v")
            for pi in range(4):
                E("scalar", lambda sc, o=sqt.ap()[:, :], i=pr2.ap()[:, pi * 512:(pi + 1) * 512]:
                  sc.activation(o, i, AF.Square), waits=[("v", now("v"))], inc="c")
                E("vector", lambda v, o=col2.ap()[:, :], i=sqt.ap()[:, :]:
                  v.reduce_sum(o, i, AXX), waits=[("c", now("c"))], inc="v")
                E("vector", lambda v, o=col3.ap()[:, :], a=col3.ap()[:, :], b=col2.ap()[:, :]:
                  v.tensor_add(o, a, b), inc="v")
            _ln_cols(E, now, col1, col3, col2, col4, 2048.0)
            E("sync", lambda s, o=y1tile.ap()[:, :], i=d_y1[t0:t0 + 128, :]:
              s.dma_start(out=o, in_=i), waits=[("v", now("v")), ("gy", now("gy"))],
              inc="sd", amt=16)
            YL = now("sd")
            E("vector", lambda v, o=pr2.ap()[:, :], i=pr2.ap()[:, :],
              s1=col1.ap()[:, 0:1], s2=col4.ap()[:, 0:1]:
              v.tensor_scalar(o, i, s1, s2, ALU.subtract, ALU.mult),
              waits=[("sd", YL), ("v", now("v"))], inc="v")
            E("vector", lambda v, o=pr2.ap()[:, :], a=pr2.ap()[:, :], b=y1tile.ap()[:, :]:
              v.tensor_add(o, a, b), inc="v")
            OV2 = now("v")
            E("gpsimd", lambda g, o=out[t0:t0 + 128, :], i=pr2.ap()[:, :]:
              g.dma_start(out=o, in_=i), waits=[("v", OV2)], inc="gd", amt=16)
            gv = now("v")
            E("tensor", lambda t: t.nop(), waits=[("v", gv)], inc="t")

    _dumpable.update(qs=d_qs, ks=d_ks, vs=d_vs, zas=d_zas, zgs=d_zgs,
                     ogl=d_ogl, agin=d_agin, agout=d_agout, ogf=d_ogf,
                     hidden=d_hid, dnb=d_dnb, y1d=d_y1)
    for nm in [x for x in os.environ.get("KDUMP", "").split(",") if x]:
        t = _dumpable[nm]
        od = nc.dram_tensor("dbg_" + nm, list(t.shape), t.dtype,
                            kind="ExternalOutput").ap()
        E("gpsimd", lambda g, o=od.opt(), i=t.opt(): g.dma_start(out=o, in_=i),
          waits=[("gd", now("gd")), ("v", now("v")), ("c", now("c")),
                 ("t", now("t")), ("g", now("g"))], inc="gd", amt=16)
    FIN = now("gd")
    E("gpsimd", lambda g: g.nop(), waits=[("gd", FIN)], inc="g")

    with nc.Block() as block:
        def replay(name):
            def body(eng):
                last = {}
                for fn, waits, inc, amt in PLAN[name]:
                    for s_, val in waits:
                        if last.get(s_, -1) < val:
                            eng.wait_ge(sems[s_], val)
                            last[s_] = val
                    inst = fn(eng)
                    if inc is not None:
                        if inst is None:
                            inst = eng.nop()
                        inst.then_inc(sems[inc], amt)
            return body
        block.sync(replay("sync"))
        block.tensor(replay("tensor"))
        block.vector(replay("vector"))
        block.scalar(replay("scalar"))
        block.gpsimd(replay("gpsimd"))

    ctx.close()
    pctx.close()
    return nc


def _ln_cols(E, now, csum, csq, ctmp, crout, n):
    E("scalar", lambda sc, o=csum.ap()[:, :], i=csum.ap()[:, :]: sc.mul(o, i, 1.0 / n),
      waits=[("v", now("v"))], inc="c")
    E("scalar", lambda sc, o=csq.ap()[:, :], i=csq.ap()[:, :]: sc.mul(o, i, 1.0 / n), inc="c")
    E("scalar", lambda sc, o=ctmp.ap()[:, :], i=csum.ap()[:, :]:
      sc.activation(o, i, AF.Square), inc="c")
    E("vector", lambda v, o=csq.ap()[:, :], a=csq.ap()[:, :], b=ctmp.ap()[:, :]:
      v.tensor_sub(o, a, b), waits=[("c", now("c"))], inc="v")
    E("scalar", lambda sc, o=ctmp.ap()[:, :], i=csq.ap()[:, :], b_=_EPSAP[0]:
      sc.activation(o, i, AF.Ln, bias=b_), waits=[("v", now("v"))], inc="c")
    E("scalar", lambda sc, o=crout.ap()[:, :], i=ctmp.ap()[:, :]:
      sc.activation(o, i, AF.Exp, scale=-0.5), inc="c")
    E("vector", lambda v: v.engine_nop(), waits=[("c", now("c"))], inc="v")


_EPSAP = [None]
_NC_CACHE = None


def kernel(**inputs):
    global _NC_CACHE
    maps = _in_maps(inputs)
    if _NC_CACHE is None:
        _NC_CACHE = build_nc()
    res = run_bass_kernel_spmd(_NC_CACHE, maps, core_ids=list(range(8)))
    outa = np.empty((B, S, HID), np.float32)
    for core in range(8):
        b, half = core // 2, core % 2
        outa[b, half * TOKH:(half + 1) * TOKH] = res.results[core]["out"]
    if os.environ.get("KDUMP"):
        kernel.dumps = [{k: v for k, v in r.items() if k.startswith("dbg_")}
                        for r in res.results]
    return outa


